# version 03
# baseline (speedup 1.0000x reference)
import numpy as np
import contextlib
import concourse.bass as bass
import concourse.mybir as mybir
from concourse.bass_utils import run_bass_kernel_spmd

F32 = mybir.dt.float32
BF16 = mybir.dt.bfloat16
AX = mybir.AxisListType.X
AF = mybir.ActivationFunctionType
ALU = mybir.AluOpType

TOK, CDIM, HID = 2048, 2048, 5632
NT, NJ, NH = 16, 16, 16
MAGIC = 12582912.0

(S_SUM, S_SQ, S_MX, S_MN, S_M, S_VAR, S_R, S_MXC, S_SA, S_DQA, S_RSA,
 S_NMR, S_NMRSA, S_DQ3, S_MXH, S_SH) = range(16)


class Prog:
    ENGINES = ("sync", "vector", "scalar", "tensor", "gpsimd")

    def __init__(self):
        self.q = {e: [] for e in self.ENGINES}
        self.sems = {}
        self.cnt = {}

    def emit(self, eng, fn):
        self.q[eng].append(fn)

    def wait(self, eng, sname, val):
        if val and val > 0:
            self.emit(eng, lambda e, s=self.sems[sname], v=val: e.wait_ge(s, v))

    def do(self, eng, build, inc=None, n=1):
        if inc is None:
            self.emit(eng, build)
            return None
        self.cnt[inc] += n
        sem = self.sems[inc]

        def run(e, b=build, s=sem, k=n):
            b(e).then_inc(s, k)
        self.emit(eng, run)
        return self.cnt[inc]

    def c(self, name):
        return self.cnt[name]


def build_nc():
    nc = bass.Bass("TRN2", target_bir_lowering=False)

    x_d = nc.dram_tensor("x", [TOK, CDIM], F32, kind="ExternalInput").ap()
    wqt_d = nc.dram_tensor("wqt", [CDIM, CDIM], F32, kind="ExternalInput").ap()
    wkt_d = nc.dram_tensor("wkt", [CDIM, CDIM], F32, kind="ExternalInput").ap()
    wvt_d = nc.dram_tensor("wvt", [CDIM, CDIM], F32, kind="ExternalInput").ap()
    wot_d = nc.dram_tensor("wot", [CDIM, CDIM], F32, kind="ExternalInput").ap()
    w1t_d = nc.dram_tensor("w1t", [CDIM, HID], F32, kind="ExternalInput").ap()
    w2t_d = nc.dram_tensor("w2t", [CDIM, HID], F32, kind="ExternalInput").ap()
    w3t_d = nc.dram_tensor("w3t", [HID, CDIM], F32, kind="ExternalInput").ap()
    wsc_d = nc.dram_tensor("wsc", [128, 3 * 11264], F32, kind="ExternalInput").ap()
    idf_d = nc.dram_tensor("idf", [128, 128], F32, kind="ExternalInput").ap()
    out_d = nc.dram_tensor("out", [TOK, CDIM], F32, kind="ExternalOutput").ap()

    h_d = nc.dram_tensor("h_scr", [HID, TOK], BF16).ap()
    y2_d = nc.dram_tensor("y2_scr", [TOK, CDIM], BF16).ap()
    kvin_d = nc.dram_tensor("cc_kv_in", [128, 2064], F32).ap()
    kvout_d = nc.dram_tensor("cc_kv_out", [128, 2064], F32).ap()
    wsin_d = nc.dram_tensor("cc_ws_in", [128, 4], F32).ap()
    wsout_d = nc.dram_tensor("cc_ws_out", [128, 4], F32).ap()

    A0 = 16512
    B0 = A0 + 65536
    C0 = B0 + 65536

    def at(name, shape, dtype, off):
        h = nc.alloc_sbuf_tensor_at(name, shape, dtype, offset=off)
        per = int(np.prod(shape[1:])) * mybir.dt.size(dtype)
        assert off >= 16512 and off + per <= 229344, (name, off, off + per)
        return h

    xnT = at("xnT", [128, NJ, TOK], BF16, A0)
    ApreT = at("ApreT", [128, NJ, TOK], BF16, A0)
    out3 = at("out3", [128, 8, CDIM], F32, A0)
    mxacc = at("mxacc", [128, CDIM], F32, A0)
    h1t = at("h1t", [128, 512], F32, A0 + 8192)
    h2t = at("h2t", [128, 512], F32, A0 + 10240)
    qT = at("qT", [128, NJ, TOK], BF16, B0)
    KTs = at("KTs", [128, NJ, TOK], BF16, B0)
    wsst = at("wsst", [128, 2, 1024], F32, B0)
    wkb = at("wkb", [128, NJ, 512], BF16, B0)
    wvb = at("wvb", [128, NJ, 512], BF16, B0 + 16384)
    wstg = at("wstg", [128, NJ, 512], F32, B0 + 32768)
    t3b = at("t3b", [128, 8, CDIM], BF16, B0)
    khb = at("khb", [128, 8, 1024], BF16, B0 + 32768)
    hinb = at("hinb", [128, 8, 1024], BF16, B0 + 49152)
    c0 = C0
    c1 = C0 + 16384
    c2 = C0 + 24576
    c3 = C0 + 32768
    c4 = C0 + 40960
    c5 = C0 + 49152
    c6 = C0 + 65536
    cP = C0 + 73728
    xtile = at("xtile", [128, 2, CDIM], F32, c0)
    wfst = at("wfst", [128, 2, NJ, 128], F32, c0)
    wf3 = at("wf3", [128, 2, CDIM], F32, c0)
    xsqd = at("xsqd", [128, CDIM], F32, c1)
    kvsb = at("kvsb", [128, NH, 128], F32, c1)
    kpre = at("kpre", [128, CDIM], F32, c1)
    wostg = at("wostg", [128, 2, 4, 256], F32, c1)
    xtl2 = at("xtl2", [128, CDIM], F32, c1)
    kvbf = at("kvbf", [128, NH, 129], BF16, c2)
    ktok = at("ktok", [128, 2, CDIM], BF16, c2)
    qtmp = at("qtmp", [128, CDIM], F32, c2)
    shr = at("shr", [1, CDIM], F32, c2)
    xnb = at("xnb", [128, 2, CDIM], BF16, c3)
    wob = at("wob", [128, NJ, 256], BF16, c3)
    outb = at("outb", [128, CDIM], F32, c3)
    kbuf = at("kbuf", [128, 2, 512], BF16, c4)
    vbuf = at("vbuf", [128, 2, 512], BF16, c4 + 2048)
    apre = at("apre", [128, 2, CDIM], BF16, c4)
    y2pc = at("y2pc", [128, 2, 256], BF16, c4)
    hrow = at("hrow", [128, 2, CDIM], BF16, c4)
    y2in = at("y2in", [128, 2, CDIM], BF16, c4)
    elut = at("elut", [128, 2, 512], F32, c5)
    elue = at("elue", [128, 2, 512], F32, c5 + 4096)
    wqb = at("wqb", [128, 2, NJ, 128], BF16, c5 + 8192)
    t1b = at("t1b", [128, 2, NJ, 128], BF16, c5)
    t2b = at("t2b", [128, 2, NJ, 128], BF16, c5 + 8192)
    bcsh = at("bcsh", [128, CDIM], F32, c6)
    stats = at("stats", [128, 256], F32, cP)
    swcols = at("swcols", [128, 16], F32, cP + 1024)
    swrow = at("swrow", [1, 16], F32, cP + 1088)
    ksb = at("ksb", [128, 16], F32, cP + 1152)
    onesb = at("onesb", [128, 1], BF16, cP + 1216)
    onesf_r = at("onesf_r", [1, 128], F32, cP + 1248)
    onesf_c = at("onesf_c", [128, 1], F32, cP + 1760)
    identf = at("identf", [128, 128], F32, cP + 1792)
    identb = at("identb", [128, 128], BF16, cP + 2304)
    zsb = at("zsb", [128, 2, 16], F32, cP + 2560)
    rows = at("rows", [16, 128], F32, cP + 2688)
    nmag = at("nmag", [128, 1], F32, cP + 3200)
    czero = at("czero", [128, 1], F32, cP + 3232)
    cone = at("cone", [128, 1], F32, cP + 3264)
    nc.const_aps.register(F32, 0.0, czero[:])
    nc.const_aps.register(F32, 1.0, cone[:])

    P = Prog()
    stack = contextlib.ExitStack()
    ps = stack.enter_context(nc.psum_tensor("ps", [128, 4096], F32))
    for s in ("sW", "sXa", "sXb", "sM", "sHa", "sHb", "sHl", "sYa", "sYb",
              "sYl", "sOa", "sOb", "cc1", "cc2", "vC", "aC", "tC", "sI",
              "sKv"):
        P.sems[s] = stack.enter_context(nc.semaphore(s))
        P.cnt[s] = 0

    def st(slot, i=None):
        if i is None:
            return stats[:, slot * 16:(slot + 1) * 16]
        return stats[:, slot * 16 + i: slot * 16 + i + 1]

    def pbank(b, w=1):
        return ps[:, b * 512:(b + w) * 512]

    W, D = P.wait, P.do

    D("gpsimd", lambda e: e.dma_start(out=identf[:], in_=idf_d), inc="sI", n=16)
    for fn in (lambda e: e.memset(czero[:], 0.0),
               lambda e: e.memset(cone[:], 1.0),
               lambda e: e.memset(onesb[:], 1.0),
               lambda e: e.memset(onesf_r[:], 1.0),
               lambda e: e.memset(onesf_c[:], 1.0),
               lambda e: e.memset(swcols[:], 0.0),
               lambda e: e.memset(stats[:], 0.0),
               lambda e: e.memset(nmag[:], -MAGIC)):
        t = D("vector", fn, inc="vC")
        W("vector", "vC", t)
    W("vector", "sI", 16)
    v_identb = D("vector", lambda e: e.tensor_copy(identb[:], identf[:]), inc="vC")
    v_init = v_identb
    W("vector", "vC", v_init)

    ws_red = {}
    for i in range(33):
        if i >= 1:
            W("sync", "vC", ws_red[i - 1])
        d = D("sync", lambda e, i=i: e.dma_start(
            out=wsst[:, i % 2], in_=wsc_d[:, i * 1024:(i + 1) * 1024]),
            inc="sW", n=16)
        W("vector", "sW", d)
        r1 = D("vector", lambda e, i=i: e.reduce_sum(
            swcols[:, 8:9], wsst[:, i % 2], axis=AX,
            apply_absolute_value=True), inc="vC")
        W("vector", "vC", r1)
        w = i // 11
        ws_red[i] = D("vector", lambda e, w=w: e.tensor_tensor(
            swcols[:, w:w + 1], swcols[:, w:w + 1], swcols[:, 8:9], ALU.add),
            inc="vC")
        W("vector", "vC", ws_red[i])
    W("gpsimd", "vC", ws_red[32])
    g1 = D("gpsimd", lambda e: e.dma_start(out=wsin_d, in_=swcols[:, 0:4]),
           inc="sM", n=16)
    W("gpsimd", "sM", g1)
    cc2_done = D("gpsimd", lambda e: e.collective_compute(
        "AllReduce", ALU.add, replica_groups=[list(range(8))],
        ins=[wsin_d.opt()], outs=[wsout_d.opt()]), inc="cc2")
    W("gpsimd", "cc2", cc2_done)
    ws_back = D("gpsimd", lambda e: e.dma_start(out=swcols[:, 0:4], in_=wsout_d),
                inc="sM", n=16)

    x0_dma, p0_v, p0_a = {}, {}, {}
    for i in range(NT):
        if i >= 2:
            W("gpsimd", "vC", p0_v[i - 2])
            W("gpsimd", "aC", p0_a[i - 2])
        x0_dma[i] = D("gpsimd", lambda e, i=i: e.dma_start(
            out=xtile[:, i % 2], in_=x_d[i * 128:(i + 1) * 128, :]),
            inc="sXa" if i % 2 == 0 else "sXb", n=16)
        W("scalar", "sXa" if i % 2 == 0 else "sXb", x0_dma[i])
        if i == 0:
            W("scalar", "vC", v_init)
        if i >= 1:
            W("scalar", "aC", p0_a[i - 1])
        aid = D("scalar", lambda e, i=i: e.activation(
            xsqd[:], xtile[:, i % 2], AF.Identity, accum_out=st(S_SUM, i)),
            inc="aC")
        W("scalar", "aC", aid)
        p0_a[i] = D("scalar", lambda e, i=i: e.activation(
            xsqd[:], xtile[:, i % 2], AF.Square, accum_out=st(S_SQ, i)),
            inc="aC")
        W("vector", "sXa" if i % 2 == 0 else "sXb", x0_dma[i])
        D("vector", lambda e, i=i: e.reduce_max(
            st(S_MX, i), xtile[:, i % 2], axis=AX), inc="vC")
        p0_v[i] = D("vector", lambda e, i=i: e.tensor_reduce(
            st(S_MN, i), xtile[:, i % 2], axis=AX, op=ALU.min), inc="vC")

    W("vector", "aC", p0_a[NT - 1])
    W("vector", "vC", p0_v[NT - 1])

    def colop(build):
        t = D("vector", build, inc="vC")
        W("vector", "vC", t)
        return t

    colop(lambda e: e.tensor_scalar(st(S_M), st(S_SUM), 1.0 / CDIM, None, ALU.mult))
    colop(lambda e: e.tensor_scalar(st(S_VAR), st(S_SQ), 1.0 / CDIM, None, ALU.mult))
    colop(lambda e: e.tensor_tensor(st(S_SQ), st(S_M), st(S_M), ALU.mult))
    colop(lambda e: e.tensor_tensor(st(S_VAR), st(S_VAR), st(S_SQ), ALU.subtract))
    v_var = colop(lambda e: e.tensor_scalar(st(S_VAR), st(S_VAR), 1e-5, None,
                                            ALU.add))
    W("scalar", "aC", p0_a[NT - 1])
    W("scalar", "vC", v_var)
    a_sq = D("scalar", lambda e: e.activation(st(S_SQ), st(S_VAR), AF.Sqrt),
             inc="aC")
    W("vector", "aC", a_sq)
    colop(lambda e: e.reciprocal(st(S_R), st(S_SQ)))
    colop(lambda e: e.tensor_tensor(st(S_MX), st(S_MX), st(S_M), ALU.subtract))
    colop(lambda e: e.tensor_tensor(st(S_MN), st(S_M), st(S_MN), ALU.subtract))
    colop(lambda e: e.tensor_tensor(st(S_MX), st(S_MX), st(S_MN), ALU.max))
    colop(lambda e: e.tensor_tensor(st(S_MXC), st(S_MX), st(S_R), ALU.mult))
    colop(lambda e: e.tensor_scalar(st(S_MXC), st(S_MXC), 1e-5, None, ALU.max))
    colop(lambda e: e.reciprocal(st(S_SA), st(S_MXC)))
    colop(lambda e: e.tensor_scalar(st(S_SA), st(S_SA), 127.0, None, ALU.mult))
    colop(lambda e: e.tensor_scalar(st(S_DQA), st(S_MXC), 1.0 / 127.0, None,
                                    ALU.mult))
    colop(lambda e: e.tensor_tensor(st(S_RSA), st(S_R), st(S_SA), ALU.mult))
    colop(lambda e: e.scalar_tensor_tensor(st(S_NMR), st(S_M), -1.0, st(S_R),
                                           ALU.mult, ALU.mult))
    p0_cols = colop(lambda e: e.scalar_tensor_tensor(
        st(S_NMRSA), st(S_M), -1.0, st(S_RSA), ALU.mult, ALU.mult))

    trs = {"gidx": 0, "ev": {}}

    def transpose_tile(src_ap, dst, tcol):
        for g in range(4):
            gi = trs["gidx"] + g
            b = gi % 8
            if gi - 8 in trs["ev"]:
                W("tensor", "vC", trs["ev"][gi - 8])
            for jj in range(4):
                j = g * 4 + jj
                kw = dict(inc="tC") if jj == 3 else {}
                D("tensor", lambda e, j=j, b=b, jj=jj: e.matmul(
                    ps[:, b * 512 + jj * 128: b * 512 + jj * 128 + 128],
                    src_ap[:, j * 128:(j + 1) * 128], identb[:],
                    start=(jj == 0), stop=(jj == 3)), **kw)
            W("vector", "tC", P.c("tC"))
            trs["ev"][gi] = D("vector", lambda e, g=g, b=b: e.tensor_copy(
                dst[:, g * 4:(g + 1) * 4, tcol:tcol + 128],
                pbank(b).rearrange("p (a c) -> p a c", a=4)), inc="vC")
        trs["gidx"] += 4
        return P.c("tC"), trs["ev"][trs["gidx"] - 1]

    W("gpsimd", "vC", p0_v[NT - 1])
    W("gpsimd", "aC", p0_a[NT - 1])
    p1_xn, p1_tr = {}, {}
    for i in range(NT):
        if i >= 2:
            W("gpsimd", "aC", p1_xn[i - 2])
        d = D("gpsimd", lambda e, i=i: e.dma_start(
            out=xtile[:, i % 2], in_=x_d[i * 128:(i + 1) * 128, :]),
            inc="sXa" if i % 2 == 0 else "sXb", n=16)
        W("scalar", "sXa" if i % 2 == 0 else "sXb", d)
        if i == 0:
            W("scalar", "vC", p0_cols)
        if i >= 2:
            W("scalar", "tC", p1_tr[i - 2][0])
        p1_xn[i] = D("scalar", lambda e, i=i: e.activation(
            xnb[:, i % 2], xtile[:, i % 2], AF.Identity,
            bias=st(S_NMR, i), scale=st(S_R, i)), inc="aC")
        W("tensor", "aC", p1_xn[i])
        if i == 0:
            W("tensor", "vC", v_identb)
        p1_tr[i] = transpose_tile(xnb[:, i % 2], xnT, i * 128)
    p1_last_ev = p1_tr[NT - 1][1]

    kv_ev, proj_done, p2_kev, p2_vev, kvmm = {}, {}, {}, {}, {}
    last_cast = ws_red[32]
    for oc in range(4):
        for wi, (wt_d, dstb) in enumerate(((wkt_d, wkb), (wvt_d, wvb))):
            if last_cast is not None:
                W("sync", "vC", last_cast)
            for half in range(2):
                D("sync", lambda e, wt_d=wt_d, oc=oc, half=half: e.dma_start(
                    out=wstg[:, half * 8:(half + 1) * 8, :],
                    in_=wt_d[:, oc * 512:(oc + 1) * 512].rearrange(
                        "(j p) o -> p j o", p=128)[:, half * 8:(half + 1) * 8, :]),
                    inc="sW", n=16)
            W("vector", "sW", P.c("sW"))
            if oc >= 1:
                W("vector", "tC", proj_done[oc - 1])
            last_cast = D("vector", lambda e, dstb=dstb: e.tensor_copy(
                dstb[:], wstg[:]), inc="vC")
            if wi == 1:
                wv_cast = last_cast
        for t in range(NT):
            kb_, vb_ = t % 2, 2 + t % 2
            par = t % 2
            W("tensor", "vC", wv_cast)
            if oc == 0 and t == 0:
                W("tensor", "vC", p1_last_ev)
            if t >= 2:
                W("tensor", "vC", p2_kev[(oc, t - 2)])
                W("tensor", "aC", p2_vev[(oc, t - 2)])
            elif oc >= 1:
                W("tensor", "vC", p2_kev[(oc - 1, NT - 2 + t)])
                W("tensor", "aC", p2_vev[(oc - 1, NT - 2 + t)])
            if t == 0 and oc >= 2:
                W("tensor", "vC", kv_ev[oc - 2])
            for j in range(NJ):
                D("tensor", lambda e, j=j, t=t, kb_=kb_: e.matmul(
                    pbank(kb_), xnT[:, j, t * 128:(t + 1) * 128], wkb[:, j, :],
                    start=(j == 0), stop=(j == NJ - 1)),
                    **(dict(inc="tC") if j == NJ - 1 else {}))
            kproj = P.c("tC")
            for j in range(NJ):
                D("tensor", lambda e, j=j, t=t, vb_=vb_: e.matmul(
                    pbank(vb_), xnT[:, j, t * 128:(t + 1) * 128], wvb[:, j, :],
                    start=(j == 0), stop=(j == NJ - 1)),
                    **(dict(inc="tC") if j == NJ - 1 else {}))
            vproj = P.c("tC")
            proj_done[oc] = vproj
            W("vector", "tC", kproj)
            e1 = D("vector", lambda e, kb_=kb_, par=par: e.tensor_scalar(
                elut[:, par], pbank(kb_), 0.0, None, ALU.min), inc="vC")
            W("scalar", "vC", e1)
            e2 = D("scalar", lambda e, par=par: e.activation(
                elue[:, par], elut[:, par], AF.Exp), inc="aC")
            W("vector", "aC", e2)
            p2_kev[(oc, t)] = D(
                "vector", lambda e, kb_=kb_, par=par: e.scalar_tensor_tensor(
                    kbuf[:, par], pbank(kb_), 0.0, elue[:, par],
                    ALU.max, ALU.add), inc="vC")
            W("scalar", "tC", vproj)
            p2_vev[(oc, t)] = D(
                "scalar", lambda e, vb_=vb_, par=par: e.activation(
                    vbuf[:, par], pbank(vb_), AF.Identity), inc="aC")
            W("tensor", "vC", p2_kev[(oc, t)])
            W("tensor", "aC", p2_vev[(oc, t)])
            kvb = 4 if oc % 2 == 0 else 6
            for hh in range(4):
                D("tensor", lambda e, hh=hh, t=t, kvb=kvb, par=par: e.matmul(
                    ps[:, kvb * 512 + hh * 128: kvb * 512 + (hh + 1) * 128],
                    kbuf[:, par, hh * 128:(hh + 1) * 128],
                    vbuf[:, par, hh * 128:(hh + 1) * 128],
                    start=(t == 0 and hh == 0),
                    stop=(t == NT - 1 and hh == 3)))
                h = oc * 4 + hh
                D("tensor", lambda e, h=h, t=t, par=par, hh=hh, oc=oc: e.matmul(
                    ps[:, 5 * 512 + h: 5 * 512 + h + 1],
                    kbuf[:, par, hh * 128:(hh + 1) * 128], onesb[:],
                    start=(oc == 0 and t == 0 and hh == 0),
                    stop=(oc == 3 and t == NT - 1 and hh == 3)),
                    **(dict(inc="tC") if hh == 3 else {}))
            kvmm[(oc, t)] = P.c("tC")
        W("vector", "tC", kvmm[(oc, NT - 1)])
        kv_ev[oc] = D("vector", lambda e, kvb=4 if oc % 2 == 0 else 6, oc=oc:
                      e.tensor_copy(kvsb[:, oc * 4:(oc + 1) * 4, :],
                                    pbank(kvb).rearrange("p (a c) -> p a c", a=4)),
                      inc="vC")
    W("vector", "tC", kvmm[(3, NT - 1)])
    ks_ev = D("vector", lambda e: e.tensor_copy(
        ksb[:], ps[:, 5 * 512: 5 * 512 + 16]), inc="vC")

    W("gpsimd", "vC", ks_ev)
    W("gpsimd", "vC", kv_ev[3])
    D("gpsimd", lambda e: e.dma_start(
        out=kvin_d[:, 0:2048], in_=kvsb[:].rearrange("p a c -> p (a c)")),
        inc="sKv", n=16)
    g2 = D("gpsimd", lambda e: e.dma_start(
        out=kvin_d[:, 2048:2064], in_=ksb[:]), inc="sKv", n=16)
    W("gpsimd", "sKv", g2)
    cc1_done = D("gpsimd", lambda e: e.collective_compute(
        "AllReduce", ALU.add, replica_groups=[[0, 1], [2, 3], [4, 5], [6, 7]],
        ins=[kvin_d.opt()], outs=[kvout_d.opt()]), inc="cc1")
    W("gpsimd", "cc1", cc1_done)
    D("gpsimd", lambda e: e.dma_start(
        out=kvsb[:].rearrange("p a c -> p (a c)"), in_=kvout_d[:, 0:2048]),
        inc="sKv", n=16)
    g3 = D("gpsimd", lambda e: e.dma_start(
        out=ksb[:], in_=kvout_d[:, 2048:2064]), inc="sKv", n=16)
    W("vector", "sKv", g3)
    kvc1 = D("vector", lambda e: e.tensor_copy(
        kvbf[:, :, 0:128], kvsb[:]), inc="vC")
    kvbf_c = D("vector", lambda e: e.tensor_copy(
        kvbf[:, :, 128:129], ksb[:].rearrange("p (a c) -> p a c", c=1)),
        inc="vC")

    q_ev, wq_cast, qmm = {}, {}, {}
    for ot in range(NJ):
        if ot == 0:
            W("sync", "aC", p1_xn[NT - 1])
            W("sync", "vC", wv_cast)
        if ot >= 1:
            W("sync", "vC", wq_cast[ot - 1])
        for half in range(2):
            D("sync", lambda e, ot=ot, half=half: e.dma_start(
                out=wfst[:, ot % 2, half * 8:(half + 1) * 8, :],
                in_=wqt_d[:, ot * 128:(ot + 1) * 128].rearrange(
                    "(j p) o -> p j o", p=128)[:, half * 8:(half + 1) * 8, :]),
                inc="sW", n=16)
        W("vector", "sW", P.c("sW"))
        if ot >= 2:
            W("vector", "tC", qmm[ot - 2])
        wq_cast[ot] = D("vector", lambda e, ot=ot: e.tensor_copy(
            wqb[:, ot % 2], wfst[:, ot % 2]), inc="vC")
        for tc in range(4):
            gi = ot * 4 + tc
            b = gi % 8
            par = gi % 2
            W("tensor", "vC", wq_cast[ot])
            if gi >= 8:
                W("tensor", "vC", q_ev[gi - 8])
            for j in range(NJ):
                D("tensor", lambda e, j=j, ot=ot, tc=tc, b=b: e.matmul(
                    pbank(b), wqb[:, ot % 2, j, :],
                    xnT[:, j, tc * 512:(tc + 1) * 512],
                    start=(j == 0), stop=(j == NJ - 1)),
                    **(dict(inc="tC") if j == NJ - 1 else {}))
            qmm[ot] = P.c("tC")
            W("vector", "tC", qmm[ot])
            e1 = D("vector", lambda e, b=b, par=par: e.tensor_scalar(
                elut[:, par], pbank(b), 0.0, None, ALU.min), inc="vC")
            W("scalar", "vC", e1)
            e2 = D("scalar", lambda e, par=par: e.activation(
                elue[:, par], elut[:, par], AF.Exp), inc="aC")
            W("vector", "aC", e2)
            q_ev[gi] = D("vector", lambda e, b=b, par=par, ot=ot, tc=tc:
                         e.scalar_tensor_tensor(
                             qT[:, ot, tc * 512:(tc + 1) * 512], pbank(b), 0.0,
                             elue[:, par], ALU.max, ALU.add), inc="vC")
    p3_last = q_ev[63]

    W("tensor", "vC", kvbf_c)
    W("tensor", "vC", p3_last)
    p4_z, p4_tev, p4_trmm = {}, {}, {}
    for tc in range(NT):
        par = tc % 2
        ob0 = 0 if par == 0 else 4
        dbk = 4 if par == 0 else 0
        if tc >= 2:
            W("tensor", "vC", p4_tev[tc - 2][-1])
        if tc >= 1:
            W("tensor", "vC", p4_tev[tc - 1][0])
            W("tensor", "vC", p4_z[tc - 1])
        for h in range(NH):
            D("tensor", lambda e, h=h, tc=tc, dbk=dbk: e.matmul(
                ps[:, dbk * 512 + h: dbk * 512 + h + 1],
                qT[:, h, tc * 128:(tc + 1) * 128], kvbf[:, h, 128:129],
                start=(h == 0), stop=(h == NH - 1)),
                **(dict(inc="tC") if h == NH - 1 else {}))
        den_mm = P.c("tC")
        for h in range(NH):
            b = ob0 + h // 4
            D("tensor", lambda e, h=h, tc=tc, b=b: e.matmul(
                ps[:, b * 512 + (h % 4) * 128: b * 512 + (h % 4) * 128 + 128],
                qT[:, h, tc * 128:(tc + 1) * 128], kvbf[:, h, 0:128],
                start=(h % 4 == 0), stop=(h % 4 == 3)),
                **(dict(inc="tC") if h == NH - 1 else {}))
        out_mm = P.c("tC")
        W("vector", "tC", den_mm)
        z1 = D("vector", lambda e, dbk=dbk, par=par: e.tensor_scalar(
            zsb[:, par], ps[:, dbk * 512: dbk * 512 + 16], 1e-6, None, ALU.add),
            inc="vC")
        W("vector", "vC", z1)
        p4_z[tc] = D("vector", lambda e, par=par: e.reciprocal(
            zsb[:, par], zsb[:, par]), inc="vC")
        W("scalar", "tC", out_mm)
        W("scalar", "vC", p4_z[tc])
        if tc == 0:
            W("scalar", "tC", kvmm[(3, NT - 1)])
        if tc >= 2:
            W("scalar", "tC", p4_trmm[tc - 2])
        for h in range(NH):
            b = ob0 + h // 4
            kw = dict(inc="aC") if h == NH - 1 else {}
            D("scalar", lambda e, h=h, b=b, par=par: e.activation(
                apre[:, par, h * 128:(h + 1) * 128],
                ps[:, b * 512 + (h % 4) * 128: b * 512 + (h % 4) * 128 + 128],
                AF.Identity, scale=zsb[:, par, h:h + 1]), **kw)
        aev = P.c("aC")
        W("tensor", "aC", aev)
        evs = []
        for g in range(4):
            b = ob0 + g
            for jj in range(4):
                j = g * 4 + jj
                kw = dict(inc="tC") if jj == 3 else {}
                D("tensor", lambda e, j=j, b=b, jj=jj, par=par: e.matmul(
                    ps[:, b * 512 + jj * 128: b * 512 + jj * 128 + 128],
                    apre[:, par, j * 128:(j + 1) * 128], identb[:],
                    start=(jj == 0), stop=(jj == 3)), **kw)
            W("vector", "tC", P.c("tC"))
            evs.append(D("vector", lambda e, g=g, b=b, tc=tc: e.tensor_copy(
                ApreT[:, g * 4:(g + 1) * 4, tc * 128:(tc + 1) * 128],
                pbank(b).rearrange("p (a c) -> p a c", a=4)), inc="vC"))
        p4_trmm[tc] = P.c("tC")
        p4_tev[tc] = evs
    p4_last_ev = p4_tev[NT - 1][-1]
    p4_prev_ev = p4_tev[NT - 2][-1]

    W("tensor", "vC", p4_last_ev)
    W("tensor", "vC", p4_prev_ev)
    W("gpsimd", "vC", wq_cast[NJ - 1])
    W("gpsimd", "aC", p1_xn[NT - 1])
    p1b_ka, p1b_k2, p1b_tr = {}, {}, {}
    for i in range(NT):
        if i >= 2:
            W("gpsimd", "aC", p1b_ka[i - 2])
        d = D("gpsimd", lambda e, i=i: e.dma_start(
            out=xtile[:, i % 2], in_=x_d[i * 128:(i + 1) * 128, :]),
            inc="sXa" if i % 2 == 0 else "sXb", n=16)
        W("scalar", "sXa" if i % 2 == 0 else "sXb", d)
        if i >= 1:
            W("scalar", "vC", p1b_k2[i - 1])
        p1b_ka[i] = D("scalar", lambda e, i=i: e.activation(
            kpre[:], xtile[:, i % 2], AF.Identity,
            bias=st(S_NMRSA, i), scale=st(S_RSA, i)), inc="aC")
        W("vector", "aC", p1b_ka[i])
        if i >= 2:
            W("vector", "tC", p1b_tr[i - 2][0])
        k1 = D("vector", lambda e, i=i: e.tensor_scalar(
            kpre[:], kpre[:], MAGIC, MAGIC, ALU.add,
            ALU.subtract), inc="vC")
        W("vector", "vC", k1)
        p1b_k2[i] = D("vector", lambda e, i=i: e.tensor_scalar(
            ktok[:, i % 2], kpre[:], st(S_DQA, i), None, ALU.mult),
            inc="vC")
        W("tensor", "vC", p1b_k2[i])
        p1b_tr[i] = transpose_tile(ktok[:, i % 2], KTs, i * 128)
    p1b_last = p1b_tr[NT - 1][1]

    wo_cast, p5_ev, p5_st, p5_mm, p5_q = {}, {}, {}, {}, {}
    NOC = 8
    for oc in range(NOC):
        for q in range(4):
            gq = oc * 4 + q
            if gq == 0:
                W("sync", "vC", p1b_k2[NT - 1])
                W("sync", "vC", wq_cast[NJ - 1])
            if gq >= 1:
                W("sync", "aC", p5_q[gq - 1])
            D("sync", lambda e, oc=oc, q=q: e.dma_start(
                out=wostg[:, q % 2],
                in_=wot_d[:, oc * 256:(oc + 1) * 256].rearrange(
                    "(j p) o -> p j o", p=128)[:, q * 4:(q + 1) * 4, :]),
                inc="sW", n=16)
            dW = P.c("sW")
            W("scalar", "sW", dW)
            if gq == 0:
                W("scalar", "tC", p1_tr[NT - 1][0])
            if q == 0 and oc >= 1:
                W("scalar", "tC", p5_mm[oc - 1])
            p5_q[gq] = D("scalar", lambda e, q=q: e.activation(
                wob[:, q * 4:(q + 1) * 4, :].rearrange("p a b -> p (a b)"),
                wostg[:, q % 2].rearrange("p a b -> p (a b)"), AF.Identity),
                inc="aC")
        wo_cast[oc] = p5_q[oc * 4 + 3]
        for tc in range(NT):
            gi = oc * NT + tc
            b = gi % 8
            W("tensor", "aC", wo_cast[oc])
            if oc == 0 and tc == 0:
                W("tensor", "vC", p1b_last)
            if gi >= 8:
                W("tensor", "aC", p5_ev[gi - 8])
            for j in range(NJ):
                D("tensor", lambda e, j=j, tc=tc, b=b: e.matmul(
                    pbank(b)[:, 0:256], ApreT[:, j, tc * 128:(tc + 1) * 128],
                    wob[:, j, :], start=(j == 0), stop=(j == NJ - 1)),
                    **(dict(inc="tC") if j == NJ - 1 else {}))
            p5_mm[oc] = P.c("tC")
            W("scalar", "tC", p5_mm[oc])
            if gi == 0:
                W("scalar", "tC", p4_trmm[NT - 1])
            if gi >= 2:
                W("scalar", "sYa" if gi % 2 == 0 else "sYb", p5_st[gi - 2])
            p5_ev[gi] = D("scalar", lambda e, b=b, par=gi % 2: e.activation(
                y2pc[:, par], pbank(b)[:, 0:256], AF.Identity), inc="aC")
            W("gpsimd", "aC", p5_ev[gi])
            p5_st[gi] = D("gpsimd", lambda e, tc=tc, oc=oc, par=gi % 2:
                          e.dma_start(
                              out=y2_d[tc * 128:(tc + 1) * 128,
                                       oc * 256:(oc + 1) * 256],
                              in_=y2pc[:, par]), inc="sYa" if gi % 2 == 0
                          else "sYb", n=16)
    p5_final_mm = P.c("tC")
    p5_final_aev = P.c("aC")

    W("tensor", "sM", ws_back)
    W("tensor", "aC", p5_final_aev)
    D("tensor", lambda e: e.matmul(
        ps[0:1, 7 * 512: 7 * 512 + 4], onesf_c[:], swcols[:, 0:4],
        start=True, stop=True), inc="tC")
    t_ws = P.c("tC")
    W("scalar", "tC", t_ws)
    a_ws = D("scalar", lambda e: e.activation(
        swrow[:, 0:4], ps[0:1, 7 * 512: 7 * 512 + 4], AF.Identity), inc="aC")
    W("vector", "aC", a_ws)
    r1 = D("vector", lambda e: e.tensor_scalar(
        swrow[:, 4:8], swrow[:, 0:4], 1.0 / (HID * CDIM), 1e-5,
        ALU.mult, ALU.max), inc="vC")
    W("vector", "vC", r1)
    r2 = D("vector", lambda e: e.reciprocal(swrow[:, 8:12], swrow[:, 4:8]),
           inc="vC")
    W("vector", "vC", r2)
    W("tensor", "vC", r2)
    for w in range(3):
        D("tensor", lambda e, w=w: e.matmul(
            ps[:, 7 * 512 + 16 + w: 7 * 512 + 16 + w + 1],
            onesf_r[:], swrow[:, 8 + w: 8 + w + 1], start=(w == 0),
            stop=False))
        D("tensor", lambda e, w=w: e.matmul(
            ps[:, 7 * 512 + 24 + w: 7 * 512 + 24 + w + 1],
            onesf_r[:], swrow[:, 4 + w: 4 + w + 1], start=False,
            stop=(w == 2)),
            **(dict(inc="tC") if w == 2 else {}))
    t_bc = P.c("tC")
    W("vector", "tC", t_bc)
    sw_c1 = D("vector", lambda e: e.tensor_copy(
        swcols[:, 4:7], ps[:, 7 * 512 + 16: 7 * 512 + 19]), inc="vC")
    sw_c2 = D("vector", lambda e: e.tensor_copy(
        swcols[:, 9:12], ps[:, 7 * 512 + 24: 7 * 512 + 27]), inc="vC")

    W("vector", "tC", p5_final_mm)
    mx_init = D("vector", lambda e: e.memset(mxacc[:], 0.0), inc="vC")

    fa_t1, fa_t2, fa_hd, fa_mm, hst, fa_q2 = {}, {}, {}, {}, {}, {}
    for m in range(44):
        par = m % 2
        if m == 0:
            W("sync", "aC", p1b_ka[NT - 1])
            W("sync", "aC", p5_q[NOC * 4 - 1])
        if m >= 1:
            W("sync", "vC", fa_q2[m - 1])
        for half in range(2):
            D("sync", lambda e, m=m, half=half: e.dma_start(
                out=wfst[:, m % 2, half * 8:(half + 1) * 8, :],
                in_=w1t_d[:, m * 128:(m + 1) * 128].rearrange(
                    "(j p) h -> p j h", p=128)[:, half * 8:(half + 1) * 8, :]),
                inc="sW", n=16)
        d1 = P.c("sW")
        W("vector", "sW", d1)
        if m == 0:
            W("vector", "vC", sw_c2)
            W("vector", "tC", p1b_tr[NT - 1][0])
        if m >= 2:
            W("vector", "tC", fa_mm[m - 2])
        wa = wfst[:, m % 2].rearrange("p a b -> p (a b)")
        q1 = D("vector", lambda e, wa=wa: e.tensor_scalar(
            qtmp[:], wa, swcols[:, 4:5], MAGIC, ALU.mult, ALU.add), inc="vC")
        W("scalar", "vC", q1)
        a1 = D("scalar", lambda e: e.activation(
            qtmp[:], qtmp[:], AF.Identity, bias=nmag[:]), inc="aC")
        W("vector", "aC", a1)
        fa_t1[m] = D("vector", lambda e, m=m: e.tensor_scalar(
            t1b[:, m % 2].rearrange("p a b -> p (a b)"), qtmp[:],
            -1.0, 1.0, ALU.max, ALU.min), inc="vC")
        W("sync", "vC", q1)
        for half in range(2):
            D("sync", lambda e, m=m, half=half: e.dma_start(
                out=wfst[:, m % 2, half * 8:(half + 1) * 8, :],
                in_=w2t_d[:, m * 128:(m + 1) * 128].rearrange(
                    "(j p) h -> p j h", p=128)[:, half * 8:(half + 1) * 8, :]),
                inc="sW", n=16)
        d2 = P.c("sW")
        W("vector", "sW", d2)
        W("vector", "vC", fa_t1[m])
        q2 = D("vector", lambda e, wa=wa: e.tensor_scalar(
            qtmp[:], wa, swcols[:, 5:6], MAGIC, ALU.mult, ALU.add), inc="vC")
        fa_q2[m] = q2
        W("scalar", "vC", q2)
        a2 = D("scalar", lambda e: e.activation(
            qtmp[:], qtmp[:], AF.Identity, bias=nmag[:]), inc="aC")
        W("vector", "aC", a2)
        fa_t2[m] = D("vector", lambda e, m=m: e.tensor_scalar(
            t2b[:, m % 2].rearrange("p a b -> p (a b)"), qtmp[:],
            -1.0, 1.0, ALU.max, ALU.min), inc="vC")
        for tc in range(4):
            it = m * 4 + tc
            by1, by2 = it % 2, 2 + it % 2
            W("tensor", "vC", fa_t2[m])
            if it == 0:
                W("tensor", "vC", mx_init)
            if it >= 2:
                W("tensor", "aC", fa_hd[it - 2][0])
                W("tensor", "vC", fa_hd[it - 2][1])
            for j in range(NJ):
                D("tensor", lambda e, j=j, m=m, tc=tc, by1=by1: e.matmul(
                    pbank(by1), t1b[:, m % 2, j, :],
                    KTs[:, j, tc * 512:(tc + 1) * 512],
                    start=(j == 0), stop=(j == NJ - 1)),
                    **(dict(inc="tC") if j == NJ - 1 else {}))
            mm1 = P.c("tC")
            for j in range(NJ):
                D("tensor", lambda e, j=j, m=m, tc=tc, by2=by2: e.matmul(
                    pbank(by2), t2b[:, m % 2, j, :],
                    KTs[:, j, tc * 512:(tc + 1) * 512],
                    start=(j == 0), stop=(j == NJ - 1)),
                    **(dict(inc="tC") if j == NJ - 1 else {}))
            mm2 = P.c("tC")
            fa_mm[m] = mm2
            W("scalar", "tC", mm1)
            if it >= 1:
                W("scalar", "vC", fa_hd[it - 1][1])
            s1 = D("scalar", lambda e, by1=by1: e.activation(
                h1t[:], pbank(by1), AF.Silu, scale=swcols[:, 9:10]), inc="aC")
            W("scalar", "tC", mm2)
            s2 = D("scalar", lambda e, by2=by2: e.activation(
                h2t[:], pbank(by2), AF.Identity, scale=swcols[:, 10:11]),
                inc="aC")
            W("vector", "aC", s2)
            if m >= 2:
                W("vector", "sHa" if m % 2 == 0 else "sHb", hst[m - 2])
            if it == 0:
                W("vector", "sYa", p5_st[126])
                W("vector", "sYb", p5_st[127])
            hv = D("vector", lambda e, m=m, tc=tc: e.tensor_tensor(
                hrow[:, m % 2, tc * 512:(tc + 1) * 512], h1t[:], h2t[:],
                ALU.mult), inc="vC")
            W("scalar", "vC", hv)
            habs = D("scalar", lambda e, m=m, tc=tc: e.activation(
                h1t[:], hrow[:, m % 2, tc * 512:(tc + 1) * 512], AF.Abs),
                inc="aC")
            W("vector", "aC", habs)
            mxv = D("vector", lambda e, tc=tc: e.tensor_tensor(
                mxacc[:, tc * 512:(tc + 1) * 512],
                mxacc[:, tc * 512:(tc + 1) * 512], h1t[:], ALU.max),
                inc="vC")
            fa_hd[it] = (s2, mxv, hv)
        W("gpsimd", "vC", fa_hd[m * 4 + 3][1])
        hst[m] = D("gpsimd", lambda e, m=m: e.dma_start(
            out=h_d[m * 128:(m + 1) * 128, :], in_=hrow[:, m % 2]),
            inc="sHa" if m % 2 == 0 else "sHb", n=16)
    fa_last_mm = fa_mm[43]
    fa_last_aev = P.c("aC")

    W("tensor", "vC", fa_hd[44 * 4 - 1][1])
    W("tensor", "aC", fa_last_aev)
    fold_ev = {}
    for bi in range(16):
        b = bi % 4
        if bi >= 4:
            W("tensor", "vC", fold_ev[bi - 4])
        D("tensor", lambda e, bi=bi, b=b: e.matmul(
            ps[:, b * 512: b * 512 + 128],
            mxacc[:, bi * 128:(bi + 1) * 128], identf[:],
            start=True, stop=True), inc="tC")
        W("vector", "tC", P.c("tC"))
        fold_ev[bi] = D("vector", lambda e, bi=bi, b=b: e.reduce_max(
            st(S_MXH, bi), ps[:, b * 512: b * 512 + 128], axis=AX), inc="vC")
    W("vector", "vC", fold_ev[15])
    c1_ = D("vector", lambda e: e.tensor_scalar(
        st(S_MXH), st(S_MXH), 1e-5, None, ALU.max), inc="vC")
    W("vector", "vC", c1_)
    d3 = D("vector", lambda e: e.tensor_scalar(
        st(S_DQ3), st(S_MXH), 1.0 / 127.0, swcols[:, 11:12],
        ALU.mult, ALU.mult), inc="vC")
    shc = D("vector", lambda e: e.reciprocal(st(S_SH), st(S_MXH)), inc="vC")
    W("vector", "vC", shc)
    shc2 = D("vector", lambda e: e.tensor_scalar(
        st(S_SH), st(S_SH), 127.0, None, ALU.mult), inc="vC")
    W("tensor", "vC", shc2)
    D("tensor", lambda e: e.matmul(
        ps[0:16, 4 * 512: 4 * 512 + 128], st(S_SH), identf[:],
        start=True, stop=True), inc="tC")
    t_sh = P.c("tC")
    W("scalar", "tC", t_sh)
    a_shr = D("scalar", lambda e: e.activation(
        rows[:], ps[0:16, 4 * 512: 4 * 512 + 128], AF.Identity), inc="aC")
    W("gpsimd", "aC", a_shr)
    W("gpsimd", "tC", p1b_tr[NT - 1][0])
    W("gpsimd", "vC", d3)
    g_shr = D("gpsimd", lambda e: e.dma_start(
        out=shr[:], in_=rows[:]), inc="sM", n=16)
    W("tensor", "sM", g_shr)
    bcsh_ev = {}
    for q in range(4):
        bq = 5 + q % 2
        if q >= 2:
            W("tensor", "vC", bcsh_ev[q - 2])
        D("tensor", lambda e, q=q, bq=bq: e.matmul(
            pbank(bq), onesf_r[:], shr[:, q * 512:(q + 1) * 512],
            start=True, stop=True), inc="tC")
        W("vector", "tC", P.c("tC"))
        bcsh_ev[q] = D("vector", lambda e, q=q, bq=bq: e.tensor_copy(
            bcsh[:, q * 512:(q + 1) * 512], pbank(bq)), inc="vC")
    bcsh_done = bcsh_ev[3]

    MACROS = [(0, 1024), (1024, 1024), (2048, 1024), (3072, 1024),
              (4096, 1024), (5120, 512)]
    NMAC = len(MACROS) - 1
    out_st = {}
    prev_mm = P.c("tC")
    prev_kh_last = None
    q3_prev = None
    fb_ev_prev_half = None
    for g2 in range(2):
        tco = g2 * 1024
        fb_ev, fb_kh, fb_t3, fb_mmd = {}, {}, {}, {}
        for mi, (h0, hsz) in enumerate(MACROS):
            nsub = hsz // 128
            for s in range(nsub):
                if q3_prev is not None:
                    W("sync", "vC", q3_prev)
                if g2 == 0 and mi == 0 and s == 0:
                    W("sync", "vC", fa_t2[43])
                d = D("sync", lambda e, h0=h0, s=s: e.dma_start(
                    out=wf3[:, s % 2],
                    in_=w3t_d[h0 + s * 128: h0 + (s + 1) * 128, :]),
                    inc="sW", n=16)
                W("vector", "sW", d)
                if prev_mm is not None:
                    W("vector", "tC", prev_mm)
                q3 = D("vector", lambda e, s=s: e.tensor_scalar(
                    qtmp[:], wf3[:, s % 2], swcols[:, 6:7], MAGIC,
                    ALU.mult, ALU.add), inc="vC")
                q3_prev = q3
                W("scalar", "vC", q3)
                a3 = D("scalar", lambda e: e.activation(
                    qtmp[:], qtmp[:], AF.Identity, bias=nmag[:]), inc="aC")
                W("vector", "aC", a3)
                fb_t3[s] = D("vector", lambda e, s=s: e.tensor_scalar(
                    t3b[:, s, :], qtmp[:], -1.0, 1.0, ALU.max, ALU.min),
                    inc="vC")
                if g2 == 0 and mi == 0 and s == 0:
                    W("gpsimd", "tC", fa_last_mm)
                    W("gpsimd", "sHa", hst[42])
                    W("gpsimd", "sHb", hst[43])
                if prev_kh_last is not None:
                    W("gpsimd", "vC", prev_kh_last)
                dh = D("gpsimd", lambda e, h0=h0, s=s, tco=tco: e.dma_start(
                    out=hinb[:, s, :],
                    in_=h_d[h0 + s * 128: h0 + (s + 1) * 128,
                            tco:tco + 1024]), inc="sHl", n=16)
                dh_all = dh + 16 * (nsub - 1 - s)
                W("vector", "sHl", dh_all)
                W("vector", "vC", bcsh_done)
                W("vector", "vC", fb_t3[s])
                qh = D("vector", lambda e, s=s, tco=tco: e.tensor_tensor(
                    qtmp[:, 0:1024], hinb[:, s, :],
                    bcsh[:, tco:tco + 1024], ALU.mult), inc="vC")
                W("vector", "vC", qh)
                fb_kh[s] = D("vector", lambda e, s=s: e.tensor_scalar(
                    khb[:, s, :], qtmp[:, 0:1024], MAGIC, MAGIC,
                    ALU.add, ALU.subtract), inc="vC")
                W("vector", "vC", fb_kh[s])
            prev_kh_last = fb_kh[nsub - 1]
            for tc in range(8):
                gb0 = 0 if tc % 2 == 0 else 4
                W("tensor", "vC", fb_kh[nsub - 1])
                W("tensor", "vC", fb_t3[nsub - 1])
                if tc >= 2:
                    W("tensor", "vC", fb_ev[(mi, tc - 2)])
                elif mi >= 1:
                    W("tensor", "vC", fb_ev[(mi - 1, 6 + tc)])
                elif g2 == 1:
                    W("tensor", "vC", fb_ev_prev_half[6 + tc])
                else:
                    if tc == 0:
                        W("tensor", "vC", fold_ev[15])
                        W("tensor", "vC", bcsh_done)
                        W("tensor", "aC", a_shr)
                for s in range(nsub):
                    for oq in range(4):
                        kw = (dict(inc="tC")
                              if (s == nsub - 1 and oq == 3) else {})
                        D("tensor", lambda e, s=s, oq=oq, tc=tc, gb0=gb0,
                          nsub=nsub:
                          e.matmul(
                              pbank(gb0 + oq),
                              khb[:, s, tc * 128:(tc + 1) * 128],
                              t3b[:, s, oq * 512:(oq + 1) * 512],
                              start=(s == 0), stop=(s == nsub - 1)), **kw)
                fb_mmd[(mi, tc)] = P.c("tC")
                W("vector", "tC", fb_mmd[(mi, tc)])
                if mi == 0:
                    fb_ev[(mi, tc)] = D(
                        "vector", lambda e, tc=tc, gb0=gb0: e.tensor_copy(
                            out3[:, tc, :], pbank(gb0, 4)), inc="vC")
                else:
                    fb_ev[(mi, tc)] = D(
                        "vector", lambda e, tc=tc, gb0=gb0:
                        e.scalar_tensor_tensor(
                            out3[:, tc, :], pbank(gb0, 4), 1.0,
                            out3[:, tc, :], ALU.mult, ALU.add), inc="vC")
            prev_mm = fb_mmd[(mi, 7)]
        for tc in range(8):
            ti = g2 * 8 + tc
            if g2 == 0 and tc == 0:
                W("gpsimd", "aC", wo_cast[NOC - 1])
                W("gpsimd", "sYa", p5_st[126])
                W("gpsimd", "sYb", p5_st[127])
            if (g2, tc) != (0, 0):
                W("gpsimd", "vC", f1_prev)
            if tc >= 2:
                W("gpsimd", "vC", fin2[tc - 2])
            dx = D("gpsimd", lambda e, ti=ti: e.dma_start(
                out=xtl2[:], in_=x_d[ti * 128:(ti + 1) * 128, :]),
                inc="sXa", n=16)
            dy = D("gpsimd", lambda e, ti=ti, tc=tc: e.dma_start(
                out=y2in[:, tc % 2], in_=y2_d[ti * 128:(ti + 1) * 128, :]),
                inc="sYl", n=16)
            W("vector", "vC", fb_ev[(NMAC, tc)])
            W("vector", "sXa", dx)
            W("vector", "sYl", dy)
            if (g2, tc) != (0, 0):
                pst = out_st[(g2, tc - 1)] if tc >= 1 else out_st[(0, 7)]
                W("vector", "sOa" if (tc - 1) % 2 == 0 else "sOb", pst)
            if tc == 0:
                fin2 = {}
            f1 = D("vector", lambda e, tc=tc, ti=ti: e.scalar_tensor_tensor(
                outb[:], out3[:, tc, :], st(S_DQ3, ti),
                xtl2[:], ALU.mult, ALU.add), inc="vC")
            f1_prev = f1
            W("vector", "vC", f1)
            fin2[tc] = D("vector", lambda e, tc=tc: e.scalar_tensor_tensor(
                outb[:], y2in[:, tc % 2], 1.0, outb[:],
                ALU.mult, ALU.add), inc="vC")
            W("gpsimd", "vC", fin2[tc])
            out_st[(g2, tc)] = D("gpsimd", lambda e, ti=ti: e.dma_start(
                out=out_d[ti * 128:(ti + 1) * 128, :], in_=outb[:]),
                inc="sOa" if tc % 2 == 0 else "sOb", n=16)
        fb_ev_prev_half = {tc: fb_ev[(NMAC, tc)] for tc in range(8)}
    W("gpsimd", "sOa", P.c("sOa"))
    W("gpsimd", "sOb", P.c("sOb"))

    with nc.Block() as block:
        @block.sync
        def _(sync):
            for fn in P.q["sync"]:
                fn(sync)

        @block.vector
        def _(vector):
            for fn in P.q["vector"]:
                fn(vector)

        @block.scalar
        def _(scalar):
            for fn in P.q["scalar"]:
                fn(scalar)

        @block.tensor
        def _(tensor):
            for fn in P.q["tensor"]:
                fn(tensor)

        @block.gpsimd
        def _(gpsimd):
            for fn in P.q["gpsimd"]:
                fn(gpsimd)

    return nc


_NC_CACHE = {}


def kernel(**inputs):
    x = np.asarray(inputs["L0_attn_x"], dtype=np.float32)
    Wq = np.asarray(inputs["L0_attn_Wq"], dtype=np.float32)
    Wk = np.asarray(inputs["L0_attn_Wk"], dtype=np.float32)
    Wv = np.asarray(inputs["L0_attn_Wv"], dtype=np.float32)
    Wo = np.asarray(inputs["L0_attn_Wo"], dtype=np.float32)
    w1 = np.asarray(inputs["L0_ffn_w1"], dtype=np.float32)
    w2 = np.asarray(inputs["L0_ffn_w2"], dtype=np.float32)
    w3 = np.asarray(inputs["L0_ffn_w3"], dtype=np.float32)

    wqt = np.ascontiguousarray(Wq.T)
    wkt = np.ascontiguousarray(Wk.T)
    wvt = np.ascontiguousarray(Wv.T)
    wot = np.ascontiguousarray(Wo.T)
    w1t = np.ascontiguousarray(w1.T)
    w2t = np.ascontiguousarray(w2.T)
    w3t = np.ascontiguousarray(w3.T)
    idf = np.eye(128, dtype=np.float32)

    nsl = HID * CDIM // 8
    in_maps = []
    for c in range(8):
        b, g = c // 2, c % 2
        xs = np.ascontiguousarray(x[b, g * TOK:(g + 1) * TOK, :])
        wsc = np.concatenate(
            [w.reshape(-1)[c * nsl:(c + 1) * nsl].reshape(128, nsl // 128)
             for w in (w1, w2, w3)], axis=1)
        in_maps.append({"x": xs, "wqt": wqt, "wkt": wkt, "wvt": wvt,
                        "wot": wot, "w1t": w1t, "w2t": w2t, "w3t": w3t,
                        "wsc": np.ascontiguousarray(wsc), "idf": idf})

    if "nc" not in _NC_CACHE:
        _NC_CACHE["nc"] = build_nc()
    res = run_bass_kernel_spmd(_NC_CACHE["nc"], in_maps, core_ids=list(range(8)))
    out = np.zeros((4, 4096, CDIM), dtype=np.float32)
    for c in range(8):
        b, g = c // 2, c % 2
        out[b, g * TOK:(g + 1) * TOK, :] = res.results[c]["out"]
    return out
